# version 08 (final)
# baseline (speedup 1.0000x reference)
import numpy as np
import ml_dtypes
from contextlib import ExitStack

import concourse.bass as bass
import concourse.mybir as mybir
from concourse.bass_utils import run_bass_kernel_spmd
from concourse.alu_op_type import AluOpType as ALU

F32 = mybir.dt.float32
BF16 = mybir.dt.bfloat16
AF = mybir.ActivationFunctionType
bf16 = ml_dtypes.bfloat16

NEG_BIG = -30000.0
EXP_SHIFT = -4.0


class Cfg:
    def __init__(self, S=4096, D=4096):
        self.S = S
        self.D = D
        self.DC = D // 128
        self.T = S // 4
        self.NT = self.T // 128
        self.NSLOT = self.NT // 2
        self.TG = min(512, self.T)
        self.NTG = self.T // self.TG
        self.H0 = D // 128
        self.H1 = D // 128
        self.KVH = self.H1 // 4
        self.NK = S // 128
        self.num_remain = max(min(S, 128), S - int(S * 0.8))
        self.slot_ext = [8 * j + 8 for j in range(self.NSLOT)]


ENG2SECT = {"t": "tensor", "v": "vector", "a": "scalar", "g": "gpsimd", "s": "sync"}


class Sem:
    __slots__ = ("name", "count", "h")

    def __init__(self, name):
        self.name = name
        self.count = 0
        self.h = None


class KB:

    def __init__(self, nc):
        self.nc = nc
        self.ops = []
        self.sems = []
        self.lastwait = {}

    def sem(self, name):
        s = Sem(name)
        self.sems.append(s)
        return s

    def wait(self, eng, tok):
        if tok is None:
            return
        if isinstance(tok, (list, tuple)) and tok and isinstance(tok[0], (list, tuple, type(None))):
            for t in tok:
                self.wait(eng, t)
            return
        sem, val = tok
        key = (eng, sem.name)
        if self.lastwait.get(key, -1) >= val:
            return
        self.lastwait[key] = val
        self.ops.append((eng, lambda e, s=sem, v=val: e.wait_ge(s.h, v)))

    def op(self, eng, fn, inc=None, k=1, fence=True, dr=False):
        if eng in ("a", "v") and dr:
            self.ops.append((eng, lambda e: e.drain()))
        if eng in ("a", "v") and inc is not None and fence:
            self.ops.append((eng, lambda e, fn=fn: fn(e)))
            inc.count += k
            tok = (inc, inc.count)

            def runf(e, k=k, inc=inc):
                e.drain()
                e.nop().then_inc(inc.h, k)

            self.ops.append((eng, runf))
            return tok
        if eng == "t" and inc is not None and getattr(self, "fence_fn", None) and fence:
            self.ops.append((eng, lambda e, fn=fn: fn(e)))
            inc.count += k
            tok = (inc, inc.count)

            def runf(e, k=k, inc=inc):
                r = self.fence_fn(e)
                r.then_inc(inc.h, k)

            self.ops.append((eng, runf))
            return tok
        tok = None
        if inc is not None:
            inc.count += k
            tok = (inc, inc.count)

        def run(e, fn=fn, inc=inc, k=k):
            r = fn(e)
            if inc is not None:
                r.then_inc(inc.h, k)

        self.ops.append((eng, run))
        return tok

    def dma_ring(self, prefix, n=6):
        ring = {"sems": [self.sem(f"{prefix}{i}") for i in range(n)], "i": 0,
                "outst": [None] * n}
        return ring

    def dma(self, eng, out, in_, ring, **kw):
        i = ring["i"] % len(ring["sems"])
        ring["i"] += 1
        s = ring["sems"][i]
        if ring["outst"][i] is not None:
            self.wait(eng, ring["outst"][i])
        tok = self.op(eng, lambda e: e.dma_start(out=out, in_=in_, **kw), inc=s, k=16)
        ring["outst"][i] = tok
        return tok

    def dma_drain(self, eng, ring):
        for t in ring["outst"]:
            self.wait(eng, t)

    def emit_into(self, block):
        for letter, sect in ENG2SECT.items():
            fns = [f for (e, f) in self.ops if e == letter]
            if not fns:
                continue
            dec = getattr(block, sect)

            def body(eng, fns=fns):
                for f in fns:
                    f(eng)

            dec(body)



def tile_weight(w, used_cols=None):
    if used_cols is not None:
        w = w[:, used_cols]
    Dd, C = w.shape
    t = w.reshape(Dd // 128, 128, C // 128, 128).transpose(2, 1, 0, 3)
    return np.ascontiguousarray(t.reshape(C, Dd))


def fmj(x):
    return np.ascontiguousarray(x.T)


def own_token_index(cfg, r):
    idx = []
    for j in range(cfg.NSLOT):
        p = 4 * j + r
        idx.extend(range(p * 256, (p + 1) * 256))
    return np.array(idx)



def build_launchA(cfg, debug=False):
    nc = bass.Bass("TRN2", target_bir_lowering=False)
    D, T, DC, TG, NTG, H0 = cfg.D, cfg.T, cfg.DC, cfg.TG, cfg.NTG, cfg.H0
    QF, KF, VF, FF = H0, H0, H0, DC
    NQ = T // 256

    ap = lambda t: t.ap()
    x0 = ap(nc.dram_tensor("x0_fm", [D, T], F32, kind="ExternalInput"))
    wq = ap(nc.dram_tensor("wq_t", [QF * 128, D], F32, kind="ExternalInput"))
    wk = ap(nc.dram_tensor("wk_t", [KF * 128, D], F32, kind="ExternalInput"))
    wv = ap(nc.dram_tensor("wv_t", [VF * 128, D], F32, kind="ExternalInput"))
    wf = ap(nc.dram_tensor("wf_t", [FF * 128, D], F32, kind="ExternalInput"))
    fbias = ap(nc.dram_tensor("fbias_fm", [128, FF], F32, kind="ExternalInput"))
    lorA = ap(nc.dram_tensor("lorA", [16 * 32, D], F32, kind="ExternalInput"))
    lorB = ap(nc.dram_tensor("lorB_t", [16 * 32, D], F32, kind="ExternalInput"))
    lorAs = ap(nc.dram_tensor("lorAs", [32, D], F32, kind="ExternalInput"))
    lorBs = ap(nc.dram_tensor("lorBs_t", [32, D], F32, kind="ExternalInput"))
    gate = ap(nc.dram_tensor("gate_row", [32, 64], F32, kind="ExternalInput"))
    identb = ap(nc.dram_tensor("identb", [128, 128], BF16, kind="ExternalInput"))

    if debug:
        dbg_u = ap(nc.dram_tensor("dbg_u", [32, 8 * 256], BF16, kind="ExternalOutput"))
        dbg_aT = ap(nc.dram_tensor("dbg_aT", [128, 10 * 32 * 128], BF16, kind="ExternalOutput"))
        dbg_be = ap(nc.dram_tensor("dbg_be", [32, 8 * 4096], BF16, kind="ExternalOutput"))
        dbg_ab = ap(nc.dram_tensor("dbg_ab", [32, 4096], BF16, kind="ExternalOutput"))
    q_out = ap(nc.dram_tensor("q_fm", [QF * 128, T], BF16, kind="ExternalOutput"))
    k_out = ap(nc.dram_tensor("k_fm", [KF * 128, T], BF16, kind="ExternalOutput"))
    v_out = ap(nc.dram_tensor("v_tok", [T, VF * 128], BF16, kind="ExternalOutput"))
    g_out = ap(nc.dram_tensor("g_fm", [D, T], BF16, kind="ExternalOutput"))

    kb = KB(nc)
    sd = kb.dma_ring("sd", 6)
    sw = kb.dma_ring("sw", 4)
    st = kb.dma_ring("st", 6)
    sp = kb.sem("sp")
    sv = kb.sem("sv")
    sa = kb.sem("sa")

    with ExitStack() as ctx:
        E = ctx.enter_context
        sb = lambda n, sh, dt=BF16: E(nc.sbuf_tensor(n, sh, dt))
        ps = lambda n, sh, dt=F32: E(nc.psum_tensor(n, sh, dt))

        xn = sb("xn", [128, DC * T])
        xbuf = [sb(f"xb{i}", [128, TG], F32) for i in range(2)]
        xsq = [sb(f"xq{i}", [128, TG], F32) for i in range(2)]
        wbuf = [sb(f"wb{i}", [128, DC * 128]) for i in range(2)]
        stage = [sb(f"stg{i}", [128, TG]) for i in range(4)]
        mrow = sb("mrow", [1, T], F32)
        rstd = sb("rstd", [1, T], F32)
        nmr = sb("nmr", [1, T], F32)
        onesf = sb("onesf", [128, 1], F32)
        ones1 = sb("ones1", [1, 128], F32)
        idb = sb("idb", [128, 128], BF16)
        gaterow = sb("gaterow", [32, 64], F32)
        lbuf = [sb(f"lb{i}", [32, D]) for i in range(2)]
        accA = [sb(f"accA{i}", [32, D]) for i in range(2)]
        accAb = sb("accAb", [32, D])
        aeffT = sb("aeffT", [128, (NQ + 1) * 32 * DC])
        beff = sb("beff", [32, NQ * D])

        def beff_slice(q, c0, c1):
            return beff[:, q * D + c0: q * D + c1]
        bsh = sb("bsh", [32, D])
        u_sb = sb("u_sb", [32, NQ * 256])
        us_sb = sb("us_sb", [32, NQ * 256])

        def u_slice(q):
            return u_sb[:, q * 256:(q + 1) * 256]
        fb_sb = sb("fb_sb", [128, FF], F32)

        pstat = ps("pstat", [1, TG])
        pstat2 = ps("pstat2", [1, TG])
        pmm = [ps(f"pmm{i}", [128, TG]) for i in range(2)]
        ptr = ps("ptr", [128, 512], BF16)
        pu = ps("pu", [32, 256])
        pfen = ps("pfen", [1, 16])

        blk = E(nc.Block())
        kb.fence_fn = lambda e: e.matmul(pfen[0:1, 0:1], onesf[0:1, 0:1],
                                         onesf[0:1, 0:1], start=True, stop=True)

        kb.op("g", lambda e: e.memset(onesf[:], 1.0))
        tok_ones = kb.op("g", lambda e: e.memset(ones1[:], 1.0), inc=sv)
        t_id = kb.dma("s", idb[:], identb[:, :], ring=sd)
        t_gate = kb.dma("s", gaterow[:], gate[:, :], ring=sd)
        t_fb = kb.dma("s", fb_sb[:], fbias[:, :], ring=sd)

        xb_free = [None, None]
        xq_free = [None, None]
        for tg in range(NTG):
            sl = slice(tg * TG, (tg + 1) * TG)
            for c in range(DC):
                b = c % 2
                kb.wait("s", xb_free[b])
                tok_x = kb.dma("s", xbuf[b][:], x0[c * 128:(c + 1) * 128, sl], ring=sd)
                kb.wait("a", tok_x)
                kb.wait("a", xq_free[b])
                tok_sq = kb.op("a", lambda e, b=b: e.activation(
                    out=xsq[b][:], in_=xbuf[b][:], func=AF.Square), inc=sa)
                kb.wait("t", tok_x)
                kb.wait("t", tok_ones)
                tok1 = kb.op("t", lambda e, b=b, c=c: e.matmul(
                    pstat[:], onesf[:], xbuf[b][:], start=(c == 0), stop=(c == DC - 1)),
                    inc=sp)
                kb.wait("t", tok_sq)
                tok2 = kb.op("t", lambda e, b=b, c=c: e.matmul(
                    pstat2[:], onesf[:], xsq[b][:], start=(c == 0), stop=(c == DC - 1)),
                    inc=sp)
                xb_free[b] = tok1
                xq_free[b] = tok2
            kb.wait("v", tok2)
            tok_r1 = kb.op("v", lambda e, sl=sl: e.tensor_scalar(
                out=mrow[:, sl], in0=pstat[:], scalar1=1.0 / D, scalar2=None,
                op0=ALU.mult), inc=sv)
            tok_r2 = kb.op("v", lambda e, sl=sl: e.tensor_scalar(
                out=rstd[:, sl], in0=pstat2[:], scalar1=1.0 / D, scalar2=None,
                op0=ALU.mult), inc=sv)
            kb.op("v", lambda e, sl=sl: e.scalar_tensor_tensor(
                out=nmr[:, sl], in0=mrow[:, sl], scalar=0.0, in1=mrow[:, sl],
                op0=ALU.add, op1=ALU.mult), dr=True)
            tok_var = kb.op("v", lambda e, sl=sl: e.scalar_tensor_tensor(
                out=rstd[:, sl], in0=rstd[:, sl], scalar=1e-5, in1=nmr[:, sl],
                op0=ALU.add, op1=ALU.subtract), inc=sv, dr=True)
            kb.wait("a", tok_var)
            kb.op("a", lambda e, sl=sl: e.activation(
                out=rstd[:, sl], in_=rstd[:, sl], func=AF.Ln))
            tok_rs = kb.op("a", lambda e, sl=sl: e.activation(
                out=rstd[:, sl], in_=rstd[:, sl], func=AF.Exp, scale=-0.5), inc=sa,
                dr=True)
            kb.wait("v", tok_rs)
            tok_nm = kb.op("v", lambda e, sl=sl: e.scalar_tensor_tensor(
                out=nmr[:, sl], in0=mrow[:, sl], scalar=-1.0, in1=rstd[:, sl],
                op0=ALU.mult, op1=ALU.mult), inc=sv, dr=True)
            kb.wait("t", tok_r1)
            kb.wait("t", tok_r2)

        x2_free = list(xb_free)
        bc_free = None
        tok_xn_all = []
        for tg in range(NTG):
            sl = slice(tg * TG, (tg + 1) * TG)
            kb.wait("t", tok_nm)
            kb.wait("t", bc_free)
            kb.op("t", lambda e, sl=sl: e.matmul(
                pmm[0][:], ones1[:], rstd[:, sl], start=True, stop=True))
            tok_bc = kb.op("t", lambda e, sl=sl: e.matmul(
                pmm[1][:], ones1[:], nmr[:, sl], start=True, stop=True), inc=sp)
            for c in range(DC):
                b = c % 2
                kb.wait("s", x2_free[b])
                tok_x = kb.dma("s", xbuf[b][:], x0[c * 128:(c + 1) * 128, sl], ring=sd)
                kb.wait("v", tok_x)
                kb.wait("v", tok_bc)
                kb.op("v", lambda e, b=b: e.tensor_tensor(
                    out=xsq[b][:], in0=xbuf[b][:], in1=pmm[0][:], op=ALU.mult))
                tok_xn = kb.op("v", lambda e, b=b, c=c, tg=tg: e.tensor_tensor(
                    out=xn[:, c * T + tg * TG: c * T + (tg + 1) * TG],
                    in0=xsq[b][:], in1=pmm[1][:], op=ALU.add), inc=sv, dr=True)
                x2_free[b] = tok_xn
            bc_free = tok_xn
            tok_xn_all.append(tok_xn)
        xn_tok = tok_xn_all[-1]

        lb_free = [None, None]
        pu_free = None
        ptA_free = None

        def build_eff(src_ap, q, is_A):
            nonlocal pu_free, ptA_free
            for n in range(16):
                b = n % 2
                kb.wait("g", lb_free[b])
                tok_l = kb.dma("g", lbuf[b][:], src_ap[n * 32:(n + 1) * 32, :], ring=sw)
                kb.wait("v", t_gate)
                kb.wait("v", tok_l)
                gsl = gaterow[:, q * 16 + n: q * 16 + n + 1]
                if n == 0:
                    tok_acc = kb.op("v", lambda e, b=b, gsl=gsl: e.tensor_scalar(
                        out=accA[1][:], in0=lbuf[b][:], scalar1=gsl, scalar2=None,
                        op0=ALU.mult), inc=sv)
                else:
                    tok_acc = kb.op("v", lambda e, b=b, n=n, gsl=gsl: e.scalar_tensor_tensor(
                        out=accA[(n + 1) % 2][:], in0=lbuf[b][:], scalar=gsl,
                        in1=accA[n % 2][:], op0=ALU.mult, op1=ALU.add), inc=sv)
                lb_free[b] = tok_acc
            final = accA[0]
            if is_A:
                kb.wait("v", tok_acc)
                tok_cb = kb.op("v", lambda e, final=final: e.tensor_copy(
                    out=accAb[:], in_=final[:]), inc=sv)
                if debug and q == 0:
                    kb.wait("s", tok_cb)
                    kb.dma("s", dbg_ab[:, :D], accAb[:], ring=st)
                for c in range(DC):
                    kb.wait("t", tok_cb)
                    kb.wait("t", t_id)
                    kb.wait("t", ptA_free)
                    tok_tr = kb.op("t", lambda e, c=c: e.transpose(
                        ptr[:, 0:32], accAb[:, c * 128:(c + 1) * 128], idb[0:32, 0:32]), inc=sp)
                    kb.wait("v", tok_tr)
                    tok_ev = kb.op("v", lambda e, q=q, c=c: e.tensor_copy(
                        out=aeffT[:, (q * DC + c) * 32:(q * DC + c + 1) * 32],
                        in_=ptr[:, 0:32]), inc=sv)
                    ptA_free = tok_ev
                return tok_ev
            else:
                tok_cp = kb.op("v", lambda e, q=q, final=final: e.tensor_copy(
                    out=beff_slice(q, 0, D), in_=final[:]), inc=sv)
                return tok_cp

        eff_toks = []
        for q in range(NQ):
            eff_toks.append(build_eff(lorA, q, True))
            eff_toks.append(build_eff(lorB, q, False))

        kb.wait("g", lb_free[0])
        tok_las = kb.dma("g", lbuf[0][:], lorAs[:, :], ring=sw)
        kb.wait("v", tok_las)
        kb.wait("v", ptA_free)
        tok_cbs = kb.op("v", lambda e: e.tensor_copy(out=accAb[:], in_=lbuf[0][:]),
                        inc=sv)
        for c in range(DC):
            kb.wait("t", tok_cbs)
            kb.wait("t", t_id)
            kb.wait("t", ptA_free)
            tok_tr = kb.op("t", lambda e, c=c: e.transpose(
                ptr[:, 0:32], accAb[:, c * 128:(c + 1) * 128], idb[0:32, 0:32]), inc=sp)
            kb.wait("v", tok_tr)
            tok_as = kb.op("v", lambda e, c=c: e.tensor_copy(
                out=aeffT[:, (NQ * DC + c) * 32:(NQ * DC + c + 1) * 32],
                in_=ptr[:, 0:32]), inc=sv)
            ptA_free = tok_as
        tok_lbs = kb.dma("g", bsh[:], lorBs[:, :], ring=sw)
        eff_toks += [tok_as, tok_lbs]

        u_toks = []
        for q in range(NQ):
            for (base, dst) in ((q * DC, None), (NQ * DC, us_sb)):
                kb.wait("t", tok_as)
                kb.wait("t", pu_free)
                kb.wait("t", xn_tok)
                for c in range(DC):
                    fn = lambda e, c=c, base=base, q=q: e.matmul(
                        pu[:], aeffT[:, (base + c) * 32:(base + c + 1) * 32],
                        xn[:, c * T + q * 256: c * T + (q + 1) * 256],
                        start=(c == 0), stop=(c == DC - 1))
                    if c == DC - 1:
                        tok_u = kb.op("t", fn, inc=sp)
                    else:
                        kb.op("t", fn)
                kb.wait("v", tok_u)
                tok_ue = kb.op("v", lambda e, q=q, dst=dst: e.tensor_copy(
                    out=(u_slice(q) if dst is None else dst[:, q * 256:(q + 1) * 256]),
                    in_=pu[:]), inc=sv)
                pu_free = tok_ue
                u_toks.append(tok_ue)
        u_tok = u_toks[-1]

        w_free = [None, None]
        pm_free = [None, None]
        stage_free = [None] * 4
        nstage = [0]

        def get_stage():
            i = nstage[0] % 4
            nstage[0] += 1
            return i, stage[i]

        def proj2(wap, nfb, epilogue, stop_in_chain=True, extra_waits=None):
            for fb in range(nfb):
                b = fb % 2
                kb.wait("g", w_free[b])
                tok_w = kb.dma("g", wbuf[b][:], wap[fb * 128:(fb + 1) * 128, :], ring=sw)
                for tg in range(NTG):
                    ip = (fb * NTG + tg) % 2
                    pb = pmm[ip]
                    kb.wait("t", tok_w)
                    kb.wait("t", xn_tok)
                    kb.wait("t", pm_free[ip])
                    if extra_waits:
                        kb.wait("t", extra_waits)
                    tok_mm = None
                    for c in range(DC):
                        fn = lambda e, b=b, c=c, tg=tg, pb=pb: e.matmul(
                            pb[:], wbuf[b][:, c * 128:(c + 1) * 128],
                            xn[:, c * T + tg * TG: c * T + (tg + 1) * TG],
                            start=(c == 0), stop=(stop_in_chain and c == DC - 1))
                        if c == DC - 1 and stop_in_chain:
                            tok_mm = kb.op("t", fn, inc=sp)
                        else:
                            kb.op("t", fn)
                    pm_free[ip] = epilogue(fb, tg, pb, tok_mm)
                w_free[b] = (sp, sp.count)

        def qk_ep(out_ap):
            def ep(fb, tg, pb, tok_mm, out_ap=out_ap):
                i, sgb = get_stage()
                kb.wait("v", tok_mm)
                kb.wait("v", stage_free[i])
                tok_e = kb.op("v", lambda e, pb=pb, sgb=sgb: e.tensor_copy(
                    out=sgb[:], in_=pb[:]), inc=sv)
                kb.wait("s", tok_e)
                stage_free[i] = kb.dma(
                    "s", out_ap[fb * 128:(fb + 1) * 128, tg * TG:(tg + 1) * TG],
                    sgb[:], ring=st)
                return tok_e
            return ep

        ptr_free = [None]

        def v_ep(fb, tg, pb, tok_mm):
            i, sgb = get_stage()
            kb.wait("v", tok_mm)
            kb.wait("v", stage_free[i])
            tok_e = kb.op("v", lambda e, pb=pb, sgb=sgb: e.tensor_copy(
                out=sgb[:], in_=pb[:]), inc=sv)
            nsub = TG // 128
            for ii in range(nsub):
                kb.wait("t", tok_e)
                kb.wait("t", t_id)
                kb.wait("t", ptr_free[0])
                tok_tr = kb.op("t", lambda e, ii=ii, sgb=sgb: e.transpose(
                    ptr[:, ii * 128:(ii + 1) * 128], sgb[:, ii * 128:(ii + 1) * 128],
                    idb[:]), inc=sp)
            i2, sgb2 = get_stage()
            kb.wait("v", tok_tr)
            kb.wait("v", stage_free[i2])
            tok_e2 = kb.op("v", lambda e, sgb2=sgb2, nsub=nsub: e.tensor_copy(
                out=sgb2[:, :nsub * 128], in_=ptr[:, :nsub * 128]), inc=sv)
            ptr_free[0] = tok_e2
            kb.wait("s", tok_e2)
            stage_free[i2] = kb.dma(
                "s", v_out[tg * TG:(tg + 1) * TG, fb * 128:(fb + 1) * 128].rearrange(
                    "(i p) f -> p i f", p=128),
                sgb2[:, :TG].rearrange("p (i f) -> p i f", f=128), ring=st)
            stage_free[i] = tok_tr
            return tok_e

        def ffn_ep(fb, tg, pb, tok_mm):
            tok_d = None
            for h in range(TG // 256):
                q = (tg * TG) // 256 + h
                kb.op("t", lambda e, fb=fb, h=h, q=q, pb=pb: e.matmul(
                    pb[:, h * 256:(h + 1) * 256],
                    beff_slice(q, fb * 128, (fb + 1) * 128),
                    u_slice(q), start=False, stop=False))
                tok_d = kb.op("t", lambda e, fb=fb, h=h, q=q, pb=pb: e.matmul(
                    pb[:, h * 256:(h + 1) * 256], bsh[:, fb * 128:(fb + 1) * 128],
                    us_sb[:, q * 256:(q + 1) * 256], start=False,
                    stop=(h == TG // 256 - 1)), inc=sp)
            i, sgb = get_stage()
            kb.wait("a", tok_d)
            kb.wait("a", t_fb)
            kb.wait("a", stage_free[i])
            tok_g = kb.op("a", lambda e, fb=fb, pb=pb, sgb=sgb: e.activation(
                out=sgb[:], in_=pb[:], func=AF.Silu, bias=fb_sb[:, fb:fb + 1]), inc=sa)
            kb.wait("s", tok_g)
            stage_free[i] = kb.dma(
                "s", g_out[fb * 128:(fb + 1) * 128, tg * TG:(tg + 1) * TG], sgb[:], ring=st)
            return tok_g

        lora_done_v = (sv, sv.count)
        if debug:
            kb.wait("s", lora_done_v)
            kb.dma("s", dbg_u[:, :NQ * 256], u_sb[:], ring=st)
            kb.dma("s", dbg_u[:, 4 * 256:4 * 256 + NQ * 256], us_sb[:], ring=st)
            kb.dma("s", dbg_aT[:, :(NQ + 1) * 32 * DC], aeffT[:], ring=st)
            kb.dma("s", dbg_be[:, :NQ * D], beff[:], ring=st)
            kb.dma("s", dbg_be[:, 4 * 4096:4 * 4096 + D], bsh[:], ring=st)
        proj2(wq, QF, qk_ep(q_out))
        proj2(wk, KF, qk_ep(k_out))
        proj2(wv, VF, v_ep)
        proj2(wf, FF, ffn_ep, stop_in_chain=False, extra_waits=[lora_done_v, tok_lbs])

        kb.dma_drain("s", st)

        for s in kb.sems:
            s.h = E(nc.semaphore(s.name))
        kb.emit_into(blk)
    return nc



def prep_launchA_inputs(cfg, inputs):
    D, H0 = cfg.D, cfg.H0
    wq_full = np.asarray(inputs["L0_attn_wq"])
    used = np.concatenate([np.arange(h * 192, h * 192 + 128) for h in range(H0)])
    wkv = np.asarray(inputs["L0_attn_wkv"])
    kcols = np.arange(0, H0 * 128)
    vcols = np.arange(H0 * 128, H0 * 256)
    shared = {
        "wq_t": tile_weight(wq_full, used),
        "wk_t": tile_weight(wkv, kcols),
        "wv_t": tile_weight(wkv, vcols),
        "wf_t": tile_weight(np.asarray(inputs["L0_ffn_kernel"])),
        "fbias_fm": np.ascontiguousarray(
            np.asarray(inputs["L0_ffn_bias"]).reshape(cfg.DC, 128).T),
        "lorA": np.asarray(inputs["L0_ffn_lora_A"]).reshape(16 * 32, D),
        "lorB_t": np.ascontiguousarray(
            np.asarray(inputs["L0_ffn_lora_B"]).transpose(0, 2, 1)).reshape(16 * 32, D),
        "lorAs": np.asarray(inputs["L0_ffn_lora_A_shared"]),
        "lorBs_t": np.ascontiguousarray(np.asarray(inputs["L0_ffn_lora_B_shared"]).T),
        "identb": np.eye(128, dtype=np.float32).astype(bf16),
    }
    return shared


def run_launchA(cfg, inputs, x0, debug=False):
    shared = prep_launchA_inputs(cfg, inputs)
    gate = np.asarray(inputs["L0_ffn_gate"])
    nc = build_launchA(cfg, debug=debug)
    in_maps = []
    for c in range(8):
        b, r = divmod(c, 4)
        idx = own_token_index(cfg, r)
        m = dict(shared)
        m["x0_fm"] = fmj(x0[b][idx])
        grow = np.zeros((1, 64), np.float32)
        for j in range(cfg.NSLOT):
            qg = ((4 * j + r) * 256) // (cfg.S // 4)
            grow[0, 16 * j:16 * (j + 1)] = gate[b * 4 + qg]
        m["gate_row"] = np.ascontiguousarray(np.tile(grow, (32, 1)))
        in_maps.append(m)
    res = run_bass_kernel_spmd(nc, in_maps, core_ids=list(range(8)))
    return res.results


def kernel(**inputs):
    cfg = Cfg()
    x0 = np.ascontiguousarray(np.asarray(inputs["L0_attn_hidden_states"], dtype=np.float32))
    resA = run_launchA(cfg, inputs, x0)
    resB = run_launchB(cfg, inputs, x0, resA)
    resC = run_launchC(cfg, inputs, resB)
    out = np.zeros((2, cfg.S, cfg.D), np.float32)
    for c in range(8):
        b, r = divmod(c, 4)
        idx = own_token_index(cfg, r)
        out[b][idx] = resC[c]["x2_fm"].T
    return out


def build_launchB(cfg, upto=99):
    nc = bass.Bass("TRN2", target_bir_lowering=False)
    D, T, DC, TG, NTG, H0 = cfg.D, cfg.T, cfg.DC, cfg.TG, cfg.NTG, cfg.H0
    S, NSLOT, KVH, H1 = cfg.S, cfg.NSLOT, cfg.KVH, cfg.H1
    SCALE0 = (128 + 64) ** -0.5
    NH = max(1, T // 512)

    ap = lambda t: t.ap()
    x0 = ap(nc.dram_tensor("x0_fm", [D, T], F32, kind="ExternalInput"))
    qfm = ap(nc.dram_tensor("q_fm", [H0 * 128, T], BF16, kind="ExternalInput"))
    kimg = ap(nc.dram_tensor("kimg", [H0 * 128, S], BF16, kind="ExternalInput"))
    vimg = ap(nc.dram_tensor("vimg", [H0 * 128, S], BF16, kind="ExternalInput"))
    gfm = ap(nc.dram_tensor("g_fm", [D, T], BF16, kind="ExternalInput"))
    masks = ap(nc.dram_tensor("masks0", [NSLOT * 8 * 128, 256], F32, kind="ExternalInput"))
    wo = ap(nc.dram_tensor("wo_t", [DC * 128, D], F32, kind="ExternalInput"))
    wq1 = ap(nc.dram_tensor("wq1_t", [H1 * 128, D], F32, kind="ExternalInput"))
    wk1 = ap(nc.dram_tensor("wk1_t", [KVH * 128, D], F32, kind="ExternalInput"))
    wv1 = ap(nc.dram_tensor("wv1_t", [KVH * 128, D], F32, kind="ExternalInput"))
    wf1 = ap(nc.dram_tensor("wf1_t", [DC * 128, D], F32, kind="ExternalInput"))
    f1b = ap(nc.dram_tensor("f1bias_fm", [128, DC], F32, kind="ExternalInput"))
    gw1 = ap(nc.dram_tensor("gw1_t", [128, DC * 4], F32, kind="ExternalInput"))
    gb1 = ap(nc.dram_tensor("gb1", [4, 1], F32, kind="ExternalInput"))
    a1t = ap(nc.dram_tensor("a1t", [128, DC * 64], F32, kind="ExternalInput"))
    b1t = ap(nc.dram_tensor("b1t", [64, D], F32, kind="ExternalInput"))
    ranksbc = ap(nc.dram_tensor("ranksbc", [128, 4], F32, kind="ExternalInput"))
    jrowbc = ap(nc.dram_tensor("jrowbc", [128, 64], F32, kind="ExternalInput"))
    iota4 = ap(nc.dram_tensor("iota4", [128, 4], F32, kind="ExternalInput"))
    cosfm = ap(nc.dram_tensor("cos_fm", [128, T], F32, kind="ExternalInput"))
    sinfm = ap(nc.dram_tensor("sin_fm", [128, T], F32, kind="ExternalInput"))
    identb = ap(nc.dram_tensor("identb", [128, 128], BF16, kind="ExternalInput"))
    identf = ap(nc.dram_tensor("identf", [128, 128], F32, kind="ExternalInput"))

    dbg_sc = ap(nc.dram_tensor("dbg_sc", [4, T], F32, kind="ExternalOutput"))
    dbg_ut = ap(nc.dram_tensor("dbg_ut", [64, T], F32, kind="ExternalOutput"))
    dbg_oh = ap(nc.dram_tensor("dbg_oh", [128, 6 * (4096 // 128)], F32, kind="ExternalOutput"))
    dbg_u = ap(nc.dram_tensor("dbg_u", [64, T], F32, kind="ExternalOutput"))
    x1_out = ap(nc.dram_tensor("x1_fm", [D, T], F32, kind="ExternalOutput"))
    qr1_out = ap(nc.dram_tensor("qr1_fm", [H1 * 128, T], BF16, kind="ExternalOutput"))
    k1_out = ap(nc.dram_tensor("k1r_fm", [KVH * 128, T], BF16, kind="ExternalOutput"))
    v1_out = ap(nc.dram_tensor("v1_tok", [T, KVH * 128], BF16, kind="ExternalOutput"))
    g1_out = ap(nc.dram_tensor("g1_fm", [D, T], BF16, kind="ExternalOutput"))

    kb = KB(nc)
    sd = kb.dma_ring("sd", 6)
    sw = kb.dma_ring("sw", 4)
    st = kb.dma_ring("st", 6)
    sp = kb.sem("sp")
    sv = kb.sem("sv")
    sa = kb.sem("sa")

    with ExitStack() as ctx:
        E = ctx.enter_context
        sb = lambda n, sh, dt=BF16: E(nc.sbuf_tensor(n, sh, dt))

        att = sb("att", [128, DC * T])
        kbuf = [sb(f"kb{i}", [128, S]) for i in range(2)]
        vbuf = [sb(f"vb{i}", [128, S]) for i in range(2)]
        qbuf = [sb(f"qb{i}", [128, T]) for i in range(2)]
        mbuf = sb("mbuf", [128, NSLOT * 8 * 256])
        ptb = [sb(f"ptb{i}", [128, 256]) for i in range(4)]
        wbuf = [sb(f"wb{i}", [128, DC * 128]) for i in range(2)]
        stage = [sb(f"stg{i}", [128, TG]) for i in range(4)]
        xbuf = [sb(f"xb{i}", [128, TG], F32) for i in range(2)]
        gbuf = [sb(f"gx{i}", [128, TG]) for i in range(2)]
        sqbuf = [sb(f"sq{i}", [128, TG], F32) for i in range(2)]
        xn1 = att
        mrow = sb("mrow", [1, T], F32)
        rstd = sb("rstd", [1, T], F32)
        nmr = sb("nmr", [1, T], F32)
        recrow = sb("recrow", [1, 256], F32)
        recbc = sb("recbc", [128, 256], F32)
        onesf = sb("onesf", [128, 1], F32)
        allonesb = sb("allonesb", [128, 128], BF16)
        ones1 = sb("ones1", [1, 128], F32)
        ebias = sb("ebias", [128, 1], F32)
        idb = sb("idb", [128, 128], BF16)
        idf = sb("idf", [128, 128], F32)
        rk_sb = sb("rk_sb", [128, 4], F32)
        ranks_sb = sb("ranks_sb", [128, 4], F32)
        jrow_sb = sb("jrow_sb", [128, 64], F32)
        iota_sb = sb("iota_sb", [128, 4], F32)
        gb1_sb = sb("gb1_sb", [4, 1], F32)
        b1t_sb = sb("b1t_sb", [64, D])
        gw_sb = sb("gw_sb", [128, DC * 4])
        a1_sb = sb("a1_sb", [128, DC * 64])
        rp1 = sb("rp1", [128, 256], F32)
        rh1 = sb("rh1", [64, TG])
        rh2 = sb("rh2", [64, TG])
        rh3 = sb("rh3", [64, TG])
        rh4 = sb("rh4", [64, TG])
        cos_lo = sb("cos_lo", [64, T], F32)
        cos_hi = sb("cos_hi", [64, T], F32)
        sin_lo = sb("sin_lo", [64, T], F32)
        sin_hi = sb("sin_hi", [64, T], F32)
        sc4 = sb("sc4", [4, T], F32)
        mcol = sb("mcol", [128, 1], F32)
        idxc = sb("idxc", [128, 1], F32)
        oh_sb = sb("oh_sb", [128, 4])
        cm_sb = sb("cm_sb", [128, 64])
        utt_sb = sb("utt_sb", [128, 64])
        ut_sb = sb("ut_sb", [64, T])
        f1b_sb = sb("f1b_sb", [128, DC], F32)
        dbgoh_sb = sb("dbgoh_sb", [128, 6 * 32], F32)

        p = [E(nc.psum_tensor(f"p{i}", [128, 512], F32)) for i in range(5)]
        p5b = E(nc.psum_tensor("p5b", [128, 1024], BF16))
        PF = E(nc.psum_tensor("pf", [128, 512], F32))
        p7 = E(nc.psum_tensor("p7", [128, 512], F32))

        blk = E(nc.Block())
        kb.fence_fn = lambda e: e.matmul(PF[0:1, 0:1], onesf[0:1, 0:1],
                                         onesf[0:1, 0:1], start=False, stop=False,
                                         skip_group_check=True)

        kb.op("g", lambda e: e.memset(onesf[:], 1.0))
        kb.op("g", lambda e: e.memset(allonesb[:], 1.0))
        kb.op("g", lambda e: e.memset(ones1[:], 1.0))
        tok_cst = kb.op("g", lambda e: e.memset(ebias[:], EXP_SHIFT), inc=sv)
        t_id = kb.dma("s", idb[:], identb[:, :], ring=sd)
        t_idf = kb.dma("s", idf[:], identf[:, :], ring=sd)
        t_rk = kb.dma("s", ranks_sb[:], ranksbc[:, :], ring=sd)
        t_jr = kb.dma("s", jrow_sb[:], jrowbc[:, :], ring=sd)
        t_io = kb.dma("s", iota_sb[:], iota4[:, :], ring=sd)
        t_gb = kb.dma("s", gb1_sb[:], gb1[:, :], ring=sd)
        t_b1 = kb.dma("g", b1t_sb[:], b1t[:, :], ring=sw)
        t_cos = kb.dma("s", cos_lo[:], cosfm[0:64, :], ring=sd)
        kb.dma("s", cos_hi[:], cosfm[64:128, :], ring=sd)
        t_sin = kb.dma("s", sin_lo[:], sinfm[0:64, :], ring=sd)
        t_sin2 = kb.dma("s", sin_hi[:], sinfm[64:128, :], ring=sd)
        t_msk = kb.dma("g", mbuf[:].rearrange("p (m q) -> p m q", q=256), masks[:, :].rearrange("(m p) q -> p m q", p=128), ring=sw)
        t_f1b = kb.dma("s", f1b_sb[:], f1b[:, :], ring=sd)
        t_gw = kb.dma("g", gw_sb[:], gw1[:, :], ring=sw)
        t_a1 = kb.dma("g", a1_sb[:], a1t[:, :], ring=sw)

        kv_free = [None, None]
        pt_free = [None] * 4
        psc_free = [None, None]
        rp1_free = None
        p2_free = None
        p3_free = None
        npt = [0]
        tok_at = None

        for h in range(H0):
            hb = h % 2
            kb.wait("g", kv_free[hb])
            tok_k = kb.dma("g", kbuf[hb][:], kimg[h * 128:(h + 1) * 128, :], ring=sw)
            tok_v = kb.dma("g", vbuf[hb][:], vimg[h * 128:(h + 1) * 128, :], ring=sw)
            kb.wait("s", kv_free[hb])
            tok_q = kb.dma("s", qbuf[hb][:], qfm[h * 128:(h + 1) * 128, :], ring=sd)
            for j in range(NSLOT):
                ext = cfg.slot_ext[j]
                tok_e = None
                tok_av = None
                for kc in range(ext):
                    ip = kc % 2
                    pb = p[ip]
                    kb.wait("t", psc_free[ip])
                    kb.wait("t", tok_k)
                    kb.wait("t", tok_q)
                    tok_s = kb.op("t", lambda e, pb=pb, hb=hb, kc=kc, j=j: e.matmul(
                        pb[:, 0:256], kbuf[hb][:, kc * 128:(kc + 1) * 128],
                        qbuf[hb][:, j * 256:(j + 1) * 256],
                        start=True, stop=True), inc=sp)
                    ipt = npt[0] % 4
                    npt[0] += 1
                    ptx = ptb[ipt]
                    mc = kc - 8 * j
                    if 0 <= mc < 8:
                        kb.wait("v", tok_s)
                        kb.wait("v", tok_cst)
                        kb.wait("v", t_msk)
                        kb.wait("v", rp1_free)
                        moff = (j * 8 + mc) * 256
                        tok_m = kb.op("v", lambda e, pb=pb, moff=moff: e.scalar_tensor_tensor(
                            out=rp1[:], in0=pb[:, 0:256], scalar=SCALE0,
                            in1=mbuf[:, moff:moff + 256], op0=ALU.mult, op1=ALU.add),
                            inc=sv)
                        kb.wait("a", tok_m)
                        kb.wait("a", pt_free[ipt])
                        tok_e = kb.op("a", lambda e, ptx=ptx: e.activation(
                            out=ptx[:], in_=rp1[:], func=AF.Exp, bias=ebias[:, 0:1]),
                            inc=sa)
                        rp1_free = tok_e
                        psc_free[ip] = tok_m
                    else:
                        kb.wait("a", tok_s)
                        kb.wait("a", tok_cst)
                        kb.wait("a", pt_free[ipt])
                        tok_e = kb.op("a", lambda e, pb=pb, ptx=ptx: e.activation(
                            out=ptx[:], in_=pb[:, 0:256], func=AF.Exp, scale=SCALE0,
                            bias=ebias[:, 0:1]), inc=sa)
                        psc_free[ip] = tok_e
                    kb.wait("t", tok_e)
                    if kc == 0:
                        kb.wait("t", p2_free)
                        kb.wait("t", p3_free)
                        kb.wait("t", tok_v)
                    kb.op("t", lambda e, ptx=ptx, kc=kc, ext=ext: e.matmul(
                        p[2][:, 0:256], allonesb[:], ptx[:],
                        start=(kc == 0), stop=(kc == ext - 1)))
                    tok_av = kb.op("t", lambda e, ptx=ptx, hb=hb, kc=kc, ext=ext: e.matmul(
                        p[3][:, 0:256], vbuf[hb][:, kc * 128:(kc + 1) * 128], ptx[:],
                        start=(kc == 0), stop=(kc == ext - 1)), inc=sp)
                    pt_free[ipt] = tok_av
                kb.wait("a", tok_av)
                kb.op("a", lambda e: e.activation(
                    out=recrow[:], in_=p[2][0:1, 0:256], func=AF.Ln))
                tok_l2 = kb.op("a", lambda e: e.activation(
                    out=recrow[:], in_=recrow[:], func=AF.Exp, scale=-1.0), inc=sa,
                    dr=True)
                kb.wait("t", tok_l2)
                tok_bc = kb.op("t", lambda e: e.matmul(
                    p[2][:, 256:512], ones1[:], recrow[:], start=True, stop=True), inc=sp)
                kb.wait("v", tok_bc)
                tok_rc = kb.op("v", lambda e: e.tensor_copy(
                    out=recbc[:], in_=p[2][:, 256:512]), inc=sv)
                kb.wait("v", tok_av)
                tok_at = kb.op("v", lambda e, h=h, j=j: e.tensor_tensor(
                    out=att[:, h * T + j * 256: h * T + (j + 1) * 256],
                    in0=p[3][:, 0:256], in1=recbc[:], op=ALU.mult), inc=sv)
                p2_free = tok_rc
                p3_free = tok_at
            kv_free[hb] = tok_at
        att_tok = tok_at

        if upto < 2:
            kb.dma_drain("s", st)
            for s_ in kb.sems:
                s_.h = E(nc.semaphore(s_.name))
            kb.emit_into(blk)
            return nc
        w_free = [None, None]
        pm_free = [None, None]
        x_free = [None, None]
        g_free = [None, None]
        sq_free = [None, None]
        stage_free = [None] * 4
        nstage = [0]

        def get_stage():
            i = nstage[0] % 4
            nstage[0] += 1
            return i, stage[i]

        p47_read = None
        tok_st = None
        for fb in range(DC):
            b = fb % 2
            kb.wait("g", w_free[b])
            tok_w = kb.dma("g", wbuf[b][:], wo[fb * 128:(fb + 1) * 128, :], ring=sw)
            for tg in range(NTG):
                ip = (fb * NTG + tg) % 2
                pb = p[ip]
                kb.wait("t", tok_w)
                kb.wait("t", att_tok)
                kb.wait("t", pm_free[ip])
                tok_mm = None
                for c in range(DC):
                    fn = lambda e, b=b, c=c, tg=tg, pb=pb: e.matmul(
                        pb[:, 0:TG], wbuf[b][:, c * 128:(c + 1) * 128],
                        att[:, c * T + tg * TG: c * T + (tg + 1) * TG],
                        start=(c == 0), stop=(c == DC - 1))
                    if c == DC - 1:
                        tok_mm = kb.op("t", fn, inc=sp)
                    else:
                        kb.op("t", fn)
                kb.wait("s", x_free[b])
                tok_x = kb.dma("s", xbuf[b][:], x0[fb * 128:(fb + 1) * 128, tg * TG:(tg + 1) * TG], ring=sd)
                kb.wait("s", g_free[b])
                tok_g = kb.dma("s", gbuf[b][:], gfm[fb * 128:(fb + 1) * 128, tg * TG:(tg + 1) * TG], ring=sd)
                kb.wait("v", tok_mm)
                kb.wait("v", tok_g)
                kb.wait("v", sq_free[b])
                kb.op("v", lambda e, pb=pb, b=b: e.tensor_tensor(
                    out=sqbuf[b][:], in0=pb[:, 0:TG], in1=gbuf[b][:], op=ALU.mult))
                kb.wait("v", tok_x)
                tok_x1 = kb.op("v", lambda e, b=b: e.tensor_tensor(
                    out=xbuf[b][:], in0=xbuf[b][:], in1=sqbuf[b][:], op=ALU.add), inc=sv, dr=True)
                g_free[b] = tok_x1
                pm_free[ip] = tok_x1
                kb.wait("s", tok_x1)
                x_free[b] = kb.dma("s", x1_out[fb * 128:(fb + 1) * 128, tg * TG:(tg + 1) * TG],
                                   xbuf[b][:], ring=st)
                kb.wait("a", tok_x1)
                tok_sq = kb.op("a", lambda e, b=b: e.activation(
                    out=sqbuf[b][:], in_=xbuf[b][:], func=AF.Square), inc=sa)
                kb.wait("t", tok_x1)
                kb.wait("t", tok_sq)
                if fb == 0:
                    kb.wait("t", p47_read)
                for g2 in range(TG // 256):
                    gidx = tg * (TG // 256) + g2
                    bank = [p[2], p[3], p[4], p7][gidx]
                    kb.op("t", lambda e, b=b, g2=g2, bank=bank, fb=fb: e.matmul(
                        bank[0:1, 0:256], onesf[:],
                        xbuf[b][:, g2 * 256:(g2 + 1) * 256],
                        start=(fb == 0), stop=(fb == DC - 1)))
                    tok_st = kb.op("t", lambda e, b=b, g2=g2, bank=bank, fb=fb: e.matmul(
                        bank[32:33, 0:256], onesf[:],
                        sqbuf[b][:, g2 * 256:(g2 + 1) * 256],
                        start=(fb == 0), stop=(fb == DC - 1)), inc=sp)
                sq_free[b] = tok_st
            w_free[b] = (sp, sp.count)
        stats_tok = tok_st

        if upto < 3:
            kb.dma_drain("s", st)
            for s_ in kb.sems:
                s_.h = E(nc.semaphore(s_.name))
            kb.emit_into(blk)
            return nc
        tok_nm = None
        kb.wait("v", stats_tok)
        for gidx in range(T // 256):
            bank = [p[2], p[3], p[4], p7][gidx]
            sl = slice(gidx * 256, (gidx + 1) * 256)
            kb.op("v", lambda e, bank=bank, sl=sl: e.tensor_scalar(
                out=mrow[:, sl], in0=bank[0:1, 0:256], scalar1=1.0 / D,
                scalar2=None, op0=ALU.mult))
            kb.op("v", lambda e, bank=bank, sl=sl: e.tensor_scalar(
                out=rstd[:, sl], in0=bank[32:33, 0:256], scalar1=1.0 / D,
                scalar2=None, op0=ALU.mult))
            kb.op("v", lambda e, sl=sl: e.scalar_tensor_tensor(
                out=nmr[:, sl], in0=mrow[:, sl], scalar=0.0, in1=mrow[:, sl],
                op0=ALU.add, op1=ALU.mult), dr=True)
            tok_var = kb.op("v", lambda e, sl=sl: e.scalar_tensor_tensor(
                out=rstd[:, sl], in0=rstd[:, sl], scalar=1e-5, in1=nmr[:, sl],
                op0=ALU.add, op1=ALU.subtract), inc=sv, dr=True)
            kb.wait("a", tok_var)
            kb.op("a", lambda e, sl=sl: e.activation(
                out=rstd[:, sl], in_=rstd[:, sl], func=AF.Ln))
            tok_rs = kb.op("a", lambda e, sl=sl: e.activation(
                out=rstd[:, sl], in_=rstd[:, sl], func=AF.Exp, scale=-0.5), inc=sa,
                dr=True)
            kb.wait("v", tok_rs)
            tok_nm = kb.op("v", lambda e, sl=sl: e.scalar_tensor_tensor(
                out=nmr[:, sl], in0=mrow[:, sl], scalar=-1.0, in1=rstd[:, sl],
                op0=ALU.mult, op1=ALU.mult), inc=sv, dr=True)
        p47_read = tok_nm

        x2_free = list(x_free)
        bc_free = None
        tok_xn = None
        for tg in range(NTG):
            sl = slice(tg * TG, (tg + 1) * TG)
            kb.wait("t", tok_nm)
            kb.wait("t", bc_free)
            kb.op("t", lambda e, sl=sl: e.matmul(
                p[0][:, 0:TG], ones1[:], rstd[:, sl], start=True, stop=True))
            tok_bc = kb.op("t", lambda e, sl=sl: e.matmul(
                p[1][:, 0:TG], ones1[:], nmr[:, sl], start=True, stop=True), inc=sp)
            for c in range(DC):
                b = c % 2
                kb.wait("s", x2_free[b])
                tok_x = kb.dma("s", xbuf[b][:], x1_out[c * 128:(c + 1) * 128, sl], ring=sd)
                kb.wait("v", tok_x)
                kb.wait("v", tok_bc)
                kb.wait("v", sq_free[b])
                kb.op("v", lambda e, b=b: e.tensor_tensor(
                    out=sqbuf[b][:], in0=xbuf[b][:], in1=p[0][:, 0:TG], op=ALU.mult))
                tok_xn = kb.op("v", lambda e, b=b, c=c, tg=tg: e.tensor_tensor(
                    out=xn1[:, c * T + tg * TG: c * T + (tg + 1) * TG],
                    in0=sqbuf[b][:], in1=p[1][:, 0:TG], op=ALU.add), inc=sv, dr=True)
                x2_free[b] = tok_xn
                sq_free[b] = tok_xn
            bc_free = tok_xn
        xn1_tok = tok_xn

        if upto < 4:
            kb.dma_drain("s", st)
            for s_ in kb.sems:
                s_.h = E(nc.semaphore(s_.name))
            kb.emit_into(blk)
            return nc
        tok_gs = None
        tok_sc = None
        for half in range(NH):
            n5 = min(512, T)
            sl = slice(half * 512, half * 512 + n5)
            kb.wait("t", t_gw)
            kb.wait("t", xn1_tok)
            kb.wait("t", p47_read)
            kb.wait("t", bc_free)
            kb.wait("t", tok_sc)
            for c in range(DC):
                fn = lambda e, c=c, half=half, n5=n5: e.matmul(
                    p[4][0:4, 0:n5], gw_sb[:, c * 4:(c + 1) * 4],
                    xn1[:, c * T + half * 512: c * T + half * 512 + n5],
                    start=(c == 0), stop=(c == DC - 1))
                if c == DC - 1:
                    tok_gs = kb.op("t", fn, inc=sp)
                else:
                    kb.op("t", fn)
            kb.wait("v", tok_gs)
            kb.wait("v", t_gb)
            tok_sc = kb.op("v", lambda e, sl=sl, n5=n5: e.tensor_scalar(
                out=sc4[:, sl], in0=p[4][0:4, 0:n5], scalar1=gb1_sb[:, 0:1],
                scalar2=None, op0=ALU.add), inc=sv)
        sc_tok = tok_sc

        ut_tok = None
        p2u_free = None
        p3u_free = None
        p5b_free = None
        for it in range(T // 128):
            kb.wait("t", t_a1)
            kb.wait("t", xn1_tok)
            kb.wait("t", p2u_free)
            tok_u = None
            for c in range(DC):
                fn = lambda e, c=c, it=it: e.matmul(
                    p[2][0:128, 0:64], xn1[:, c * T + it * 128: c * T + (it + 1) * 128],
                    a1_sb[:, c * 64:(c + 1) * 64],
                    start=(c == 0), stop=(c == DC - 1))
                if c == DC - 1:
                    tok_u = kb.op("t", fn, inc=sp)
                else:
                    kb.op("t", fn)
            kb.wait("t", sc_tok)
            kb.wait("t", t_idf)
            kb.wait("t", p3u_free)
            tok_tr = kb.op("t", lambda e, it=it: e.transpose(
                p[3][0:128, 0:4], sc4[0:4, it * 128:(it + 1) * 128], idf[0:4, 0:4]),
                inc=sp)
            kb.wait("v", tok_tr)
            kb.op("v", lambda e: e.tensor_reduce(
                out=mcol[:], in_=p[3][0:128, 0:4], axis=mybir.AxisListType.X, op=ALU.max))
            p3u_free = kb.op("v", lambda e: e.tensor_scalar(
                out=rk_sb[:], in0=p[3][0:128, 0:4], scalar1=mcol[:, 0:1], scalar2=None,
                op0=ALU.is_ge), inc=sv, dr=True)
            kb.op("v", lambda e: e.tensor_copy(out=oh_sb[:, 0:1], in_=rk_sb[:, 0:1]))
            kb.op("v", lambda e: e.tensor_scalar(
                out=idxc[:], in0=rk_sb[:, 0:1], scalar1=-1.0, scalar2=1.0,
                op0=ALU.mult, op1=ALU.add))
            for kk in range(1, 4):
                kb.op("v", lambda e, kk=kk: e.tensor_tensor(
                    out=oh_sb[:, kk:kk + 1], in0=rk_sb[:, kk:kk + 1], in1=idxc[:],
                    op=ALU.mult), dr=True)
                if kk < 3:
                    kb.op("v", lambda e, kk=kk: e.scalar_tensor_tensor(
                        out=idxc[:], in0=rk_sb[:, kk:kk + 1], scalar=-1.0, in1=idxc[:],
                        op0=ALU.mult, op1=ALU.mult) if False else e.scalar_tensor_tensor(
                        out=idxc[:], in0=oh_sb[:, kk:kk + 1], scalar=-1.0, in1=idxc[:],
                        op0=ALU.mult, op1=ALU.add), dr=True)
            kb.wait("v", t_rk)
            kb.op("v", lambda e: e.tensor_tensor(
                out=rk_sb[:], in0=oh_sb[:], in1=ranks_sb[:], op=ALU.mult), dr=True)
            kb.op("v", lambda e: e.tensor_reduce(
                out=mcol[:], in_=rk_sb[:], axis=mybir.AxisListType.X, op=ALU.add), dr=True)
            kb.wait("v", t_jr)
            tok_cm = kb.op("v", lambda e: e.tensor_scalar(
                out=cm_sb[:], in0=jrow_sb[:], scalar1=mcol[:, 0:1], scalar2=None,
                op0=ALU.is_lt), inc=sv, dr=True)
            kb.op("v", lambda e, it=it: e.tensor_copy(out=dbgoh_sb[:, it * 6:it * 6 + 4], in_=oh_sb[:]))
            kb.op("v", lambda e, it=it: e.tensor_copy(out=dbgoh_sb[:, it * 6 + 4:it * 6 + 5], in_=mcol[:]))
            kb.op("v", lambda e, it=it: e.tensor_copy(out=dbgoh_sb[:, it * 6 + 5:it * 6 + 6], in_=idxc[:]))
            kb.wait("v", tok_u)
            tok_utt = kb.op("v", lambda e: e.tensor_tensor(
                out=utt_sb[:], in0=p[2][0:128, 0:64], in1=cm_sb[:], op=ALU.mult), inc=sv)
            p2u_free = tok_utt
            kb.wait("t", tok_utt)
            kb.wait("t", t_id)
            kb.wait("t", p5b_free)
            tok_ut = kb.op("t", lambda e: e.transpose(
                p5b[0:64, 0:128], utt_sb[:], idb[:]), inc=sp)
            kb.wait("v", tok_ut)
            tok_utc = kb.op("v", lambda e, it=it: e.tensor_copy(
                out=ut_sb[:, it * 128:(it + 1) * 128], in_=p5b[0:64, 0:128]), inc=sv)
            p5b_free = tok_utc
            ut_tok = tok_utc

        kb.wait("g", ut_tok)
        kb.dma("g", dbg_sc[:, :], sc4[:], ring=sw)
        kb.dma("g", dbg_ut[:, :], ut_sb[:], ring=sw)
        kb.dma("g", dbg_oh[:, 0:6 * (T // 128)], dbgoh_sb[:, 0:6 * (T // 128)], ring=sw)

        if upto < 5:
            kb.dma_drain("s", st)
            for s_ in kb.sems:
                s_.h = E(nc.semaphore(s_.name))
            kb.emit_into(blk)
            return nc
        pm_free = [None, None]
        ptr_free = None

        def proj2(wap, nfb, epilogue, stop_in_chain=True, extra=None):
            nonlocal pm_free
            for fb in range(nfb):
                b = fb % 2
                kb.wait("g", w_free[b])
                tok_w = kb.dma("g", wbuf[b][:], wap[fb * 128:(fb + 1) * 128, :], ring=sw)
                for tg in range(NTG):
                    ip = (fb * NTG + tg) % 2
                    pb = p[ip]
                    kb.wait("t", tok_w)
                    kb.wait("t", xn1_tok)
                    kb.wait("t", pm_free[ip])
                    if extra:
                        kb.wait("t", extra)
                    tok_mm = None
                    for c in range(DC):
                        fn = lambda e, b=b, c=c, tg=tg, pb=pb: e.matmul(
                            pb[:, 0:TG], wbuf[b][:, c * 128:(c + 1) * 128],
                            xn1[:, c * T + tg * TG: c * T + (tg + 1) * TG],
                            start=(c == 0), stop=(stop_in_chain and c == DC - 1))
                        if c == DC - 1 and stop_in_chain:
                            tok_mm = kb.op("t", fn, inc=sp)
                        else:
                            kb.op("t", fn)
                    pm_free[ip] = epilogue(fb, tg, pb, tok_mm)
                w_free[b] = (sp, sp.count)

        def rope_ep(out_ap):
            def ep(fb, tg, pb, tok_mm, out_ap=out_ap):
                i, sgb = get_stage()
                tsl = slice(tg * TG, (tg + 1) * TG)
                kb.wait("v", tok_mm)
                kb.wait("v", stage_free[i])
                kb.wait("v", t_cos)
                kb.wait("v", t_sin)
                kb.wait("v", t_sin2)
                tok_e = kb.op("v", lambda e, pb=pb, sgb=sgb: e.tensor_copy(
                    out=sgb[:], in_=pb[:, 0:TG]), inc=sv)
                kb.op("v", lambda e, sgb=sgb: e.tensor_copy(
                    out=rh1[:], in_=sgb[64:128, :]), dr=True)
                kb.op("v", lambda e, sgb=sgb, tsl=tsl: e.tensor_tensor(
                    out=rh2[:], in0=sgb[0:64, :], in1=cos_lo[:, tsl], op=ALU.mult))
                kb.op("v", lambda e, tsl=tsl: e.tensor_tensor(
                    out=rh3[:], in0=rh1[:], in1=sin_lo[:, tsl], op=ALU.mult), dr=True)
                kb.op("v", lambda e, tsl=tsl: e.tensor_tensor(
                    out=rh4[:], in0=rh1[:], in1=cos_hi[:, tsl], op=ALU.mult))
                kb.op("v", lambda e, sgb=sgb, tsl=tsl: e.tensor_tensor(
                    out=rh1[:], in0=sgb[0:64, :], in1=sin_hi[:, tsl], op=ALU.mult), dr=True)
                kb.op("v", lambda e, sgb=sgb: e.tensor_tensor(
                    out=sgb[0:64, :], in0=rh2[:], in1=rh3[:], op=ALU.subtract), dr=True)
                tok_r = kb.op("v", lambda e, sgb=sgb: e.tensor_tensor(
                    out=sgb[64:128, :], in0=rh4[:], in1=rh1[:], op=ALU.add), inc=sv,
                    dr=True)
                kb.wait("s", tok_r)
                stage_free[i] = kb.dma(
                    "s", out_ap[fb * 128:(fb + 1) * 128, tsl], sgb[:], ring=st)
                return tok_r
            return ep

        def v1_ep(fb, tg, pb, tok_mm):
            nonlocal ptr_free
            i, sgb = get_stage()
            kb.wait("v", tok_mm)
            kb.wait("v", stage_free[i])
            tok_e = kb.op("v", lambda e, pb=pb, sgb=sgb: e.tensor_copy(
                out=sgb[:], in_=pb[:, 0:TG]), inc=sv)
            nsub = TG // 128
            tok_tr = None
            for ii in range(nsub):
                kb.wait("t", tok_e)
                kb.wait("t", t_id)
                kb.wait("t", ptr_free)
                tok_tr = kb.op("t", lambda e, ii=ii, sgb=sgb: e.transpose(
                    p5b[0:128, 128 + ii * 128:128 + (ii + 1) * 128],
                    sgb[:, ii * 128:(ii + 1) * 128], idb[:]), inc=sp)
            i2, sgb2 = get_stage()
            kb.wait("v", tok_tr)
            kb.wait("v", stage_free[i2])
            tok_e2 = kb.op("v", lambda e, sgb2=sgb2, nsub=nsub: e.tensor_copy(
                out=sgb2[:, :nsub * 128], in_=p5b[0:128, 128:128 + nsub * 128]), inc=sv)
            ptr_free = tok_e2
            kb.wait("s", tok_e2)
            stage_free[i2] = kb.dma(
                "s", v1_out[tg * TG:(tg + 1) * TG, fb * 128:(fb + 1) * 128].rearrange(
                    "(i p) f -> p i f", p=128),
                sgb2[:, :TG].rearrange("p (i f) -> p i f", f=128), ring=st)
            stage_free[i] = tok_tr
            return tok_e

        def ffn1_ep(fb, tg, pb, tok_mm):
            kb.wait("t", ut_tok)
            kb.wait("t", t_b1)
            tok_d = kb.op("t", lambda e, fb=fb, tg=tg, pb=pb: e.matmul(
                pb[:, 0:TG], b1t_sb[:, fb * 128:(fb + 1) * 128],
                ut_sb[:, tg * TG:(tg + 1) * TG], start=False, stop=True), inc=sp)
            i, sgb = get_stage()
            kb.wait("a", tok_d)
            kb.wait("a", t_f1b)
            kb.wait("a", stage_free[i])
            tok_g = kb.op("a", lambda e, fb=fb, pb=pb, sgb=sgb: e.activation(
                out=sgb[:], in_=pb[:, 0:TG], func=AF.Silu, bias=f1b_sb[:, fb:fb + 1]), inc=sa)
            kb.wait("s", tok_g)
            stage_free[i] = kb.dma(
                "s", g1_out[fb * 128:(fb + 1) * 128, tg * TG:(tg + 1) * TG], sgb[:], ring=st)
            return tok_g

        proj2(wq1, H1, rope_ep(qr1_out))
        proj2(wk1, KVH, rope_ep(k1_out))
        proj2(wv1, KVH, v1_ep)
        proj2(wf1, DC, ffn1_ep, stop_in_chain=False, extra=ut_tok)

        kb.dma_drain("s", st)

        for s_ in kb.sems:
            s_.h = E(nc.semaphore(s_.name))
        kb.emit_into(blk)
    return nc


def prep_launchB_shared(cfg, inputs):
    D, H0, KVH, H1 = cfg.D, cfg.H0, cfg.KVH, cfg.H1
    RANKS = (8, 16, 32, 64)
    wo = np.asarray(inputs["L0_attn_wo"])
    Wq1 = np.asarray(inputs["L1_attn_Wq"])
    Wk1 = np.asarray(inputs["L1_attn_Wk"])
    Wv1 = np.asarray(inputs["L1_attn_Wv"])
    W1 = np.asarray(inputs["L1_ffn_W"])
    shared = {
        "wo_t": tile_weight(wo),
        "wq1_t": tile_weight(np.ascontiguousarray(Wq1.T)),
        "wk1_t": tile_weight(np.ascontiguousarray(Wk1.T)),
        "wv1_t": tile_weight(np.ascontiguousarray(Wv1.T)),
        "wf1_t": tile_weight(np.ascontiguousarray(W1.T)),
        "f1bias_fm": np.ascontiguousarray(
            np.asarray(inputs["L1_ffn_b"]).reshape(cfg.DC, 128).T),
        "gw1_t": np.ascontiguousarray(np.asarray(inputs["L1_ffn_gate_w"]).T.reshape(
            cfg.DC, 128, 4).transpose(1, 0, 2).reshape(128, cfg.DC * 4)),
        "gb1": np.asarray(inputs["L1_ffn_gate_b"]).reshape(4, 1),
        "a1t": np.ascontiguousarray(np.asarray(inputs["L1_ffn_A"]).T.reshape(
            cfg.DC, 128, 64).transpose(1, 0, 2).reshape(128, cfg.DC * 64)),
        "b1t": np.ascontiguousarray(np.asarray(inputs["L1_ffn_B"]).T),
        "ranksbc": np.tile(np.array(RANKS, np.float32), (128, 1)),
        "jrowbc": np.tile(np.arange(64, dtype=np.float32), (128, 1)),
        "iota4": np.tile(np.arange(4, dtype=np.float32), (128, 1)),
        "identb": np.eye(128, dtype=np.float32).astype(bf16),
        "identf": np.eye(128, dtype=np.float32),
    }
    return shared


def rope_tables(cfg):
    S, HD = cfg.S, 128
    inv = 1.0 / (10000.0 ** (np.arange(0, HD, 2, dtype=np.float32) / HD))
    t = np.arange(S, dtype=np.float32)
    fr = np.outer(t, inv)
    emb = np.concatenate([fr, fr], -1)
    return np.cos(emb).astype(np.float32), np.sin(emb).astype(np.float32)


def l0_masks(cfg, r, val=NEG_BIG):
    out = np.zeros((cfg.NSLOT, 8, 128, 256), np.float32)
    for j in range(cfg.NSLOT):
        pr = 4 * j + r
        tA, tB = 2 * pr, 2 * pr + 1
        for mc in range(8):
            kc = 8 * j + mc
            kk = kc * 128 + np.arange(128)[:, None]
            qA = tA * 128 + np.arange(128)[None, :]
            qB = tB * 128 + np.arange(128)[None, :]
            out[j, mc, :, 0:128] = np.where(kk <= qA, 0.0, val)
            out[j, mc, :, 128:256] = np.where(kk <= qB, 0.0, val)
    return out.reshape(cfg.NSLOT * 8 * 128, 256)


def run_launchB(cfg, inputs, x0, resA, upto=99):
    shared = prep_launchB_shared(cfg, inputs)
    cos, sin = rope_tables(cfg)
    S, D, H0, T = cfg.S, cfg.D, cfg.H0, cfg.T
    NK = S // 128
    res = []
    in_maps = []
    for b in range(2):
        kfull = np.zeros((H0 * 128, S), bf16)
        vfull = np.zeros((S, H0 * 128), bf16)
        for r in range(4):
            idx = own_token_index(cfg, r)
            kfull[:, idx] = resA[b * 4 + r]["k_fm"]
            vfull[idx, :] = resA[b * 4 + r]["v_tok"]
        vimg = np.ascontiguousarray(
            vfull.reshape(NK, 128, H0, 128).transpose(2, 1, 0, 3).reshape(H0 * 128, S))
        for r in range(4):
            c = b * 4 + r
            idx = own_token_index(cfg, r)
            m = dict(shared)
            m["x0_fm"] = fmj(x0[b][idx])
            m["q_fm"] = resA[c]["q_fm"]
            m["g_fm"] = resA[c]["g_fm"]
            m["kimg"] = kfull
            m["vimg"] = vimg
            m["masks0"] = l0_masks(cfg, r)
            m["cos_fm"] = np.ascontiguousarray(cos[idx].T)
            m["sin_fm"] = np.ascontiguousarray(sin[idx].T)
            in_maps.append(m)
    nc = build_launchB(cfg, upto=upto)
    out = run_bass_kernel_spmd(nc, in_maps, core_ids=list(range(8)))
    return out.results



BIG = 16384.0

def build_launchC(cfg, debug=False):
    nc = bass.Bass("TRN2", target_bir_lowering=False)
    D, T, DC, TG, NTG = cfg.D, cfg.T, cfg.DC, cfg.TG, cfg.NTG
    S, NSLOT, KVH, H1 = cfg.S, cfg.NSLOT, cfg.KVH, cfg.H1
    SC1 = 128.0 ** -0.5
    NREM = float(cfg.num_remain)
    NCH = S // 512

    ap = lambda t: t.ap()
    x1 = ap(nc.dram_tensor("x1_fm", [D, T], F32, kind="ExternalInput"))
    qr1 = ap(nc.dram_tensor("qr1_fm", [H1 * 128, T], BF16, kind="ExternalInput"))
    k1img = ap(nc.dram_tensor("k1img", [KVH * 128, S], BF16, kind="ExternalInput"))
    v1img = ap(nc.dram_tensor("v1img", [KVH * 128, S], BF16, kind="ExternalInput"))
    g1fm = ap(nc.dram_tensor("g1_fm", [D, T], BF16, kind="ExternalInput"))
    masks = ap(nc.dram_tensor("masks1", [NSLOT * 8 * 128, 256], BF16, kind="ExternalInput"))
    wo1 = ap(nc.dram_tensor("wo1_t", [DC * 128, D], F32, kind="ExternalInput"))
    p1h = ap(nc.dram_tensor("p1h", [H1 * 128, 128], F32, kind="ExternalInput"))
    p2h = ap(nc.dram_tensor("p2h", [H1 * 128, 128], F32, kind="ExternalInput"))
    b1h = ap(nc.dram_tensor("b1h", [128, H1], F32, kind="ExternalInput"))
    b2h = ap(nc.dram_tensor("b2h", [128, H1], F32, kind="ExternalInput"))
    identb = ap(nc.dram_tensor("identb", [128, 128], BF16, kind="ExternalInput"))
    negtriu = ap(nc.dram_tensor("negtriu", [128, 128], BF16, kind="ExternalInput"))
    neglow = ap(nc.dram_tensor("neglow", [128, 128], BF16, kind="ExternalInput"))
    ibig = ap(nc.dram_tensor("ibig", [128, 128], BF16, kind="ExternalInput"))
    inegbig = ap(nc.dram_tensor("inegbig", [128, 128], BF16, kind="ExternalInput"))

    if debug:
        dbg_sim = ap(nc.dram_tensor("dbg_sim", [128, 8 * 8 * 256], F32, kind="ExternalOutput"))
        dbg_sel = ap(nc.dram_tensor("dbg_sel", [128, 8 * 8 * 256], F32, kind="ExternalOutput"))
        dbg_th = ap(nc.dram_tensor("dbg_th", [128, 64], F32, kind="ExternalOutput"))
        dbg_eq = ap(nc.dram_tensor("dbg_eq", [128, 8 * 8 * 256], F32, kind="ExternalOutput"))
        dbg_z = ap(nc.dram_tensor("dbg_z", [128, 8 * 256], F32, kind="ExternalOutput"))
    x2_out = ap(nc.dram_tensor("x2_fm", [D, T], F32, kind="ExternalOutput"))

    kb = KB(nc)
    sd = kb.dma_ring("sd", 6)
    sw = kb.dma_ring("sw", 4)
    st = kb.dma_ring("st", 6)
    sp = kb.sem("sp")
    sv = kb.sem("sv")
    sa = kb.sem("sa")

    with ExitStack() as ctx:
        E = ctx.enter_context
        sb = lambda n, sh, dt=BF16: E(nc.sbuf_tensor(n, sh, dt))

        att = sb("att", [128, DC * T])
        kv_k = sb("kv_k", [128, S])
        kv_v = sb("kv_v", [128, S])
        kh_sb = sb("kh_sb", [128, S])
        qh_sb = sb("qh_sb", [128, T])
        qbuf = sb("qbuf", [128, T])
        ybuf = sb("ybuf", [128, 512])
        sim_sb = sb("sim_sb", [128, NSLOT * 8 * 256])
        eq_sb = sb("eq_sb", [128, 8 * NSLOT * 256])
        sel_sb = sb("sel_sb", [128, 8 * NSLOT * 256])
        ind_sb = [sb(f"ind{i}", [128, 256]) for i in range(4)]
        mbuf = sb("mbuf", [128, NSLOT * 8 * 256])
        tbc = [sb(f"tbc{i}", [128, 256]) for i in range(2)]
        t1bc = sb("t1bc", [128, 256])
        stepc = sb("stepc", [128, 256], F32)
        cge_sb = sb("cge_sb", [128, 256], F32)
        z0_sb = sb("z0_sb", [128, 256], F32)
        zdbg_sb = sb("zdbg_sb", [128, 8 * 256], F32) if debug else None
        ptb = [sb(f"ptb{i}", [128, 256]) for i in range(4)]
        wbuf = [sb(f"wb{i}", [128, DC * 128]) for i in range(2)]
        stage = [sb(f"stg{i}", [128, TG]) for i in range(4)]
        xbuf = [sb(f"xb{i}", [128, TG], F32) for i in range(2)]
        gbuf = [sb(f"gx{i}", [128, TG]) for i in range(2)]
        sqbuf = [sb(f"sq{i}", [128, TG], F32) for i in range(2)]
        recrow = sb("recrow", [1, 256], F32)
        recbc = sb("recbc", [128, 256], F32)
        onesf = sb("onesf", [128, 1], F32)
        allonesb = sb("allonesb", [128, 128], BF16)
        ones1 = sb("ones1", [1, 128], F32)
        ebias = sb("ebias", [128, 1], F32)
        idb = sb("idb", [128, 128], BF16)
        ntri = sb("ntri", [128, 128], BF16)
        nlow = sb("nlow", [128, 128], BF16)
        sibig = sb("sibig", [128, 128], BF16)
        sinb = sb("sinb", [128, 128], BF16)
        sp1 = sb("sp1", [128, 128])
        sp2 = sb("sp2", [128, 128])
        sb1c = sb("sb1c", [128, H1], F32)
        sb2c = sb("sb2c", [128, H1], F32)

        p = [E(nc.psum_tensor(f"p{i}", [128, 512], F32)) for i in range(6)]
        PF = E(nc.psum_tensor("pf", [128, 512], F32))
        p7 = E(nc.psum_tensor("p7", [128, 512], F32))

        blk = E(nc.Block())
        kb.fence_fn = lambda e: e.matmul(PF[0:1, 0:1], onesf[0:1, 0:1],
                                         onesf[0:1, 0:1], start=False, stop=False,
                                         skip_group_check=True)

        kb.op("g", lambda e: e.memset(onesf[:], 1.0))
        kb.op("g", lambda e: e.memset(allonesb[:], 1.0))
        kb.op("g", lambda e: e.memset(ones1[:], 1.0))
        tok_cst = kb.op("g", lambda e: e.memset(ebias[:], EXP_SHIFT), inc=sv)
        t_id = kb.dma("s", idb[:], identb[:, :], ring=sd)
        t_nt = kb.dma("s", ntri[:], negtriu[:, :], ring=sd)
        t_nl = kb.dma("s", nlow[:], neglow[:, :], ring=sd)
        t_ib = kb.dma("s", sibig[:], ibig[:, :], ring=sd)
        t_in = kb.dma("s", sinb[:], inegbig[:, :], ring=sd)
        t_b1 = kb.dma("s", sb1c[:], b1h[:, :], ring=sd)
        t_b2 = kb.dma("s", sb2c[:], b2h[:, :], ring=sd)
        t_msk = kb.dma("s", mbuf[:].rearrange("p (m q) -> p m q", q=256),
                       masks[:, :].rearrange("(m p) q -> p m q", p=128), ring=sd)

        def hash_into(dst, src_sb, ncols, h, tok_src):
            toks = []
            cs = min(512, ncols)
            for cc in range(ncols // cs):
                csl = slice(cc * cs, (cc + 1) * cs)
                kb.wait("t", tok_src)
                kb.wait("t", (sv, sv.count))
                kb.wait("t", hash_free[0])
                tok_m1 = kb.op("t", lambda e, h=h, csl=csl, cs=cs: e.matmul(
                    p7[:, 0:cs], sp1[:], src_sb[:, csl], start=True, stop=True), inc=sp)
                kb.wait("a", tok_m1)
                tok_s1 = kb.op("a", lambda e, h=h, cs=cs: e.activation(
                    out=ybuf[:, 0:cs], in_=p7[:, 0:cs], func=AF.Silu,
                    bias=sb1c[:, h:h + 1]), inc=sa)
                kb.wait("v", tok_s1)
                tok_y = kb.op("v", lambda e, csl=csl, cs=cs: e.tensor_tensor(
                    out=ybuf[:, 0:cs], in0=ybuf[:, 0:cs], in1=src_sb[:, csl],
                    op=ALU.add), inc=sv)
                kb.wait("t", tok_y)
                kb.op("t", lambda e, h=h, cs=cs: e.matmul(
                    p7[:, 0:cs], sp2[:], ybuf[:, 0:cs], start=True, stop=False))
                tok_m2 = kb.op("t", lambda e, cs=cs: e.matmul(
                    p7[:, 0:cs], idb[:], ybuf[:, 0:cs], start=False, stop=True), inc=sp)
                kb.wait("a", tok_m2)
                tok_sg = kb.op("a", lambda e, h=h, dst=dst, csl=csl, cs=cs: e.activation(
                    out=dst[:, csl], in_=p7[:, 0:cs], func=AF.Sign,
                    bias=sb2c[:, h:h + 1]), inc=sa)
                hash_free[0] = tok_sg
                toks.append(tok_sg)
            return toks[-1]

        kvb = [None]
        hash_free = [None]
        tok_at = None
        p2_free = None
        p3_free = None
        p4_free = None
        p5_free = None
        psc_free = [None, None]
        pt_free = [None] * 4
        ind_free = [None] * 4
        npt = [0]
        SRCH = [64, 32, 16, 8, 4, 2]

        for h in range(H1):
            kv = h // 4
            if h % 4 == 0:
                kb.wait("g", (sv, sv.count))
                tok_kv = kb.dma("g", kv_k[:], k1img[kv * 128:(kv + 1) * 128, :], ring=sw)
                tok_vv = kb.dma("g", kv_v[:], v1img[kv * 128:(kv + 1) * 128, :], ring=sw)
                kvb[0] = (tok_kv, tok_vv)
            tok_kv, tok_vv = kvb[0]
            kb.wait("s", (sv, sv.count))
            tok_p1 = kb.dma("g", sp1[:], p1h[h * 128:(h + 1) * 128, :], ring=sw)
            tok_p2 = kb.dma("g", sp2[:], p2h[h * 128:(h + 1) * 128, :], ring=sw)
            tok_q = kb.dma("s", qbuf[:], qr1[h * 128:(h + 1) * 128, :], ring=sd)
            kb.wait("t", tok_p1)
            kb.wait("t", tok_p2)
            kb.wait("t", t_b1)
            kb.wait("t", t_b2)
            kb.wait("t", t_id)
            tok_kh = hash_into(kh_sb, kv_k, S, h, tok_kv)
            tok_qh = hash_into(qh_sb, qbuf, T, h, tok_q)

            for j in range(NSLOT):
                ext = cfg.slot_ext[j]
                soff = 0
                evac_toks = []
                kb.wait("v", p5_free)
                for kc in range(ext):
                    ip = kc % 2
                    pb = p[ip]
                    kb.wait("t", psc_free[ip])
                    kb.wait("t", tok_kh)
                    kb.wait("t", tok_qh)
                    tok_s = kb.op("t", lambda e, pb=pb, kc=kc, j=j: e.matmul(
                        pb[:, 0:256], kh_sb[:, kc * 128:(kc + 1) * 128],
                        qh_sb[:, j * 256:(j + 1) * 256], start=True, stop=True), inc=sp)
                    mc = kc - 8 * j
                    if 0 <= mc < 8:
                        kb.wait("v", tok_s)
                        kb.wait("v", t_msk)
                        moff = (j * 8 + mc) * 256
                        tok_e = kb.op("v", lambda e, pb=pb, moff=moff, kc=kc: e.scalar_tensor_tensor(
                            out=sim_sb[:, kc * 256:(kc + 1) * 256], in0=pb[:, 0:256],
                            scalar=0.0, in1=mbuf[:, moff:moff + 256],
                            op0=ALU.add, op1=ALU.add), inc=sv)
                    else:
                        kb.wait("v", tok_s)
                        tok_e = kb.op("v", lambda e, pb=pb, kc=kc: e.tensor_copy(
                            out=sim_sb[:, kc * 256:(kc + 1) * 256], in_=pb[:, 0:256]),
                            inc=sv)
                    psc_free[ip] = tok_e
                    evac_toks.append(tok_e)
                sim_tok = evac_toks[-1]
                kb.wait("v", p4_free)
                tok_t = kb.op("v", lambda e: e.memset(tbc[0][:], -1.0), inc=sv)
                cur = 0
                for si in range(7):
                    cnt_tok = None
                    for kc in range(ext):
                        ib = kc % 4
                        kb.wait("v", ind_free[ib])
                        kb.wait("v", sim_tok)
                        kb.wait("v", tok_t)
                        tok_i = kb.op("v", lambda e, kc=kc, ib=ib, cur=cur: e.tensor_tensor(
                            out=ind_sb[ib][:], in0=sim_sb[:, kc * 256:(kc + 1) * 256],
                            in1=tbc[cur][:], op=ALU.is_ge), inc=sv)
                        kb.wait("t", tok_i)
                        if kc == 0:
                            kb.wait("t", p4_free)
                        cnt_tok = kb.op("t", lambda e, ib=ib, kc=kc, ext=ext: e.matmul(
                            p[4][:, 0:256], allonesb[:], ind_sb[ib][:],
                            start=(kc == 0), stop=(kc == ext - 1)), inc=sp,
                            fence=(kc == ext - 1))
                        ind_free[ib] = cnt_tok
                    d = SRCH[si] if si < 6 else 2.0
                    kb.wait("v", cnt_tok)
                    if si < 6:
                        kb.op("v", lambda e, d=d: e.tensor_scalar(
                            out=stepc[:], in0=p[4][:, 0:256], scalar1=NREM,
                            scalar2=2.0 * d, op0=ALU.is_ge, op1=ALU.mult))
                        tok_t = kb.op("v", lambda e, d=d, cur=cur: e.scalar_tensor_tensor(
                            out=tbc[1 - cur][:], in0=stepc[:], scalar=-d,
                            in1=tbc[cur][:], op0=ALU.add, op1=ALU.add), inc=sv, dr=True)
                    else:
                        kb.op("v", lambda e: e.tensor_scalar(
                            out=stepc[:], in0=p[4][:, 0:256], scalar1=NREM,
                            scalar2=2.0, op0=ALU.is_ge, op1=ALU.mult))
                        tok_t = kb.op("v", lambda e, cur=cur: e.scalar_tensor_tensor(
                            out=tbc[1 - cur][:], in0=stepc[:], scalar=-2.0,
                            in1=tbc[cur][:], op0=ALU.add, op1=ALU.add), inc=sv, dr=True)
                    cur = 1 - cur
                    p4_free = tok_t
                tok_t1 = kb.op("v", lambda e, cur=cur: e.tensor_scalar(
                    out=t1bc[:], in0=tbc[cur][:], scalar1=1.0, scalar2=None,
                    op0=ALU.add), inc=sv)
                ge_tok = None
                eq_tok = None
                for kc in range(ext):
                    ib = kc % 4
                    kb.wait("v", tok_t1)
                    kb.wait("v", ind_free[ib])
                    tok_i = kb.op("v", lambda e, kc=kc, ib=ib, cur=cur: e.tensor_tensor(
                        out=ind_sb[ib][:], in0=sim_sb[:, kc * 256:(kc + 1) * 256],
                        in1=tbc[cur][:], op=ALU.is_ge), inc=sv)
                    kb.wait("t", tok_i)
                    if kc == 0:
                        kb.wait("t", p4_free)
                    ge_tok = kb.op("t", lambda e, ib=ib, kc=kc, ext=ext: e.matmul(
                        p[4][:, 0:256], allonesb[:], ind_sb[ib][:],
                        start=(kc == 0), stop=(kc == ext - 1)), inc=sp,
                        fence=(kc == ext - 1))
                    ind_free[ib] = ge_tok
                    tok_q2 = kb.op("v", lambda e, kc=kc: e.tensor_tensor(
                        out=eq_sb[:, kc * 256:(kc + 1) * 256],
                        in0=sim_sb[:, kc * 256:(kc + 1) * 256], in1=t1bc[:],
                        op=ALU.is_equal), inc=sv)
                    kb.wait("t", tok_q2)
                    if kc == 0:
                        kb.wait("t", p3_free)
                    eq_tok = kb.op("t", lambda e, kc=kc, ext=ext: e.matmul(
                        p[3][:, 256:512], allonesb[:],
                        eq_sb[:, kc * 256:(kc + 1) * 256],
                        start=(kc == 0), stop=(kc == ext - 1)), inc=sp,
                        fence=(kc == ext - 1))
                kb.wait("v", ge_tok)
                kb.wait("v", eq_tok)
                kb.op("v", lambda e: e.tensor_copy(out=cge_sb[:], in_=p[4][:, 0:256]))
                kb.op("v", lambda e: e.tensor_tensor(
                    out=z0_sb[:], in0=p[3][:, 256:512], in1=cge_sb[:], op=ALU.subtract),
                    dr=True)
                kb.op("v", lambda e, cur=cur: e.scalar_tensor_tensor(
                    out=z0_sb[:], in0=tbc[cur][:], scalar=-BIG, in1=z0_sb[:],
                    op0=ALU.mult, op1=ALU.add), dr=True)
                tok_z0 = kb.op("v", lambda e: e.tensor_scalar(
                    out=z0_sb[:], in0=z0_sb[:], scalar1=NREM - BIG, scalar2=None,
                    op0=ALU.add), inc=sv, dr=True)
                p4_free = tok_z0
                p3_free = tok_z0
                sel_toks = []
                for kc in range(ext):
                    kb.wait("t", tok_z0)
                    kb.wait("t", t_nt)
                    kb.wait("t", t_ib)
                    kb.wait("t", t_nl)
                    kb.wait("t", t_in)
                    kb.wait("t", p5_free)
                    if kc == 0:
                        kb.op("t", lambda e: e.matmul(
                            p[5][:, 0:256], ones1[:], z0_sb[0:1, :], start=True,
                            stop=False, skip_group_check=True))
                    kb.op("t", lambda e, kc=kc: e.matmul(
                        p[5][:, 0:256], ntri[:], eq_sb[:, kc * 256:(kc + 1) * 256],
                        start=False, stop=False, skip_group_check=True))
                    tok_zc = kb.op("t", lambda e, kc=kc: e.matmul(
                        p[5][:, 0:256], sibig[:], sim_sb[:, kc * 256:(kc + 1) * 256],
                        start=False, stop=False, skip_group_check=True), inc=sp)
                    kb.wait("v", tok_zc)
                    if debug and h == 0 and j == 0:
                        kb.op("v", lambda e, kc=kc: e.tensor_copy(
                            out=zdbg_sb[:, kc * 256:(kc + 1) * 256], in_=p[5][:, 0:256]))
                    tok_sel = kb.op("v", lambda e, kc=kc: e.tensor_scalar(
                        out=sel_sb[:, kc * 256:(kc + 1) * 256], in0=p[5][:, 0:256],
                        scalar1=0.0, scalar2=None, op0=ALU.is_ge), inc=sv)
                    sel_toks.append(tok_sel)
                    kb.wait("t", tok_sel)
                    kb.op("t", lambda e, kc=kc: e.matmul(
                        p[5][:, 0:256], sinb[:], sim_sb[:, kc * 256:(kc + 1) * 256],
                        start=False, stop=False, skip_group_check=True))
                    p5_free = kb.op("t", lambda e, kc=kc: e.matmul(
                        p[5][:, 0:256], nlow[:], eq_sb[:, kc * 256:(kc + 1) * 256],
                        start=False, stop=False, skip_group_check=True), inc=sp)
                sel_tok = sel_toks[-1]
                tok_av = None
                for kc in range(ext):
                    ip = kc % 2
                    pb = p[ip]
                    kb.wait("t", psc_free[ip])
                    kb.wait("t", tok_vv)
                    tok_s = kb.op("t", lambda e, pb=pb, kc=kc, j=j: e.matmul(
                        pb[:, 0:256], kv_k[:, kc * 128:(kc + 1) * 128],
                        qbuf[:, j * 256:(j + 1) * 256], start=True, stop=True), inc=sp)
                    ipt = npt[0] % 4
                    npt[0] += 1
                    ptx = ptb[ipt]
                    kb.wait("a", tok_s)
                    kb.wait("a", tok_cst)
                    kb.wait("a", pt_free[ipt])
                    tok_x = kb.op("a", lambda e, pb=pb, ptx=ptx: e.activation(
                        out=ptx[:], in_=pb[:, 0:256], func=AF.Exp, scale=SC1,
                        bias=ebias[:, 0:1]), inc=sa)
                    psc_free[ip] = tok_x
                    kb.wait("v", tok_x)
                    kb.wait("v", sel_tok)
                    tok_pt = kb.op("v", lambda e, ptx=ptx, kc=kc: e.tensor_tensor(
                        out=ptx[:], in0=ptx[:], in1=sel_sb[:, kc * 256:(kc + 1) * 256],
                        op=ALU.mult), inc=sv)
                    kb.wait("t", tok_pt)
                    if kc == 0:
                        kb.wait("t", p2_free)
                        kb.wait("t", p3_free)
                    kb.op("t", lambda e, ptx=ptx, kc=kc, ext=ext: e.matmul(
                        p[2][:, 0:256], allonesb[:], ptx[:],
                        start=(kc == 0), stop=(kc == ext - 1)))
                    tok_av = kb.op("t", lambda e, ptx=ptx, kc=kc, ext=ext: e.matmul(
                        p[3][:, 0:256], kv_v[:, kc * 128:(kc + 1) * 128], ptx[:],
                        start=(kc == 0), stop=(kc == ext - 1)), inc=sp)
                    pt_free[ipt] = tok_av
                kb.wait("a", tok_av)
                kb.op("a", lambda e: e.activation(
                    out=recrow[:], in_=p[2][0:1, 0:256], func=AF.Ln))
                tok_l2 = kb.op("a", lambda e: e.activation(
                    out=recrow[:], in_=recrow[:], func=AF.Exp, scale=-1.0), inc=sa,
                    dr=True)
                kb.wait("t", tok_l2)
                tok_bc = kb.op("t", lambda e: e.matmul(
                    p[2][:, 256:512], ones1[:], recrow[:], start=True, stop=True), inc=sp)
                kb.wait("v", tok_bc)
                tok_rc = kb.op("v", lambda e: e.tensor_copy(
                    out=recbc[:], in_=p[2][:, 256:512]), inc=sv)
                kb.wait("v", tok_av)
                tok_at = kb.op("v", lambda e, h=h, j=j: e.tensor_tensor(
                    out=att[:, h * T + j * 256: h * T + (j + 1) * 256],
                    in0=p[3][:, 0:256], in1=recbc[:], op=ALU.mult), inc=sv)
                p2_free = tok_rc
                p3_free = tok_at
                if debug and h < 8:
                    kb.wait("g", sim_tok)
                    kb.wait("g", sel_tok)
                    kb.wait("g", tok_t1)
                    dsl = (h * NSLOT + j) * 8 * 256 if NSLOT == 1 else 0
                    if NSLOT == 1:
                        kb.dma("g", dbg_sim[:, dsl:dsl + ext * 256], sim_sb[:, 0:ext * 256], ring=sw)
                        kb.dma("g", dbg_sel[:, dsl:dsl + ext * 256], sel_sb[:, 0:ext * 256], ring=sw)
                        kb.dma("g", dbg_eq[:, dsl:dsl + ext * 256], eq_sb[:, 0:ext * 256], ring=sw)
                        if h == 0:
                            kb.dma("g", dbg_z[:, 0:ext * 256], zdbg_sb[:, 0:ext * 256], ring=sw)
                        kb.dma("g", dbg_th[:, h * NSLOT + j:h * NSLOT + j + 1],
                               t1bc[:, 0:1], ring=sw, allow_slow_non_contiguous=True)
        att_tok = tok_at

        w_free = [None, None]
        pm_free = [None, None]
        x_free = [None, None]
        g_free = [None, None]
        stage_free = [None] * 4
        nstage = [0]

        def get_stage():
            i = nstage[0] % 4
            nstage[0] += 1
            return i, stage[i]

        for fb in range(DC):
            b = fb % 2
            kb.wait("g", w_free[b])
            tok_w = kb.dma("g", wbuf[b][:], wo1[fb * 128:(fb + 1) * 128, :], ring=sw)
            for tg in range(NTG):
                ip = (fb * NTG + tg) % 2
                pb = p[ip]
                kb.wait("t", tok_w)
                kb.wait("t", att_tok)
                kb.wait("t", pm_free[ip])
                tok_mm = None
                for c in range(DC):
                    fn = lambda e, b=b, c=c, tg=tg, pb=pb: e.matmul(
                        pb[:, 0:TG], wbuf[b][:, c * 128:(c + 1) * 128],
                        att[:, c * T + tg * TG: c * T + (tg + 1) * TG],
                        start=(c == 0), stop=(c == DC - 1))
                    if c == DC - 1:
                        tok_mm = kb.op("t", fn, inc=sp)
                    else:
                        kb.op("t", fn)
                kb.wait("s", x_free[b])
                tok_x = kb.dma("s", xbuf[b][:], x1[fb * 128:(fb + 1) * 128, tg * TG:(tg + 1) * TG], ring=sd)
                kb.wait("s", g_free[b])
                tok_g = kb.dma("s", gbuf[b][:], g1fm[fb * 128:(fb + 1) * 128, tg * TG:(tg + 1) * TG], ring=sd)
                i, sgb = get_stage()
                kb.wait("v", tok_mm)
                kb.wait("v", tok_g)
                kb.op("v", lambda e, pb=pb, b=b: e.tensor_tensor(
                    out=sqbuf[b][:], in0=pb[:, 0:TG], in1=gbuf[b][:], op=ALU.mult))
                kb.wait("v", tok_x)
                tok_x2 = kb.op("v", lambda e, b=b: e.tensor_tensor(
                    out=xbuf[b][:], in0=xbuf[b][:], in1=sqbuf[b][:], op=ALU.add), inc=sv, dr=True)
                g_free[b] = tok_x2
                pm_free[ip] = tok_x2
                kb.wait("s", tok_x2)
                x_free[b] = kb.dma("s", x2_out[fb * 128:(fb + 1) * 128, tg * TG:(tg + 1) * TG],
                                   xbuf[b][:], ring=st)
            w_free[b] = (sp, sp.count)

        kb.dma_drain("s", st)

        for s_ in kb.sems:
            s_.h = E(nc.semaphore(s_.name))
        kb.emit_into(blk)
    return nc


def prep_launchC_shared(cfg, inputs):
    H1 = cfg.H1
    tri = np.tril(np.ones((128, 128), np.float32))
    triu_incl = np.triu(np.ones((128, 128), np.float32))
    low_strict = np.tril(np.ones((128, 128), np.float32), -1)
    shared = {
        "wo1_t": tile_weight(np.ascontiguousarray(np.asarray(inputs["L1_attn_Wo"]).T)),
        "p1h": np.asarray(inputs["L1_attn_hash_proj1"]).reshape(H1 * 128, 128),
        "p2h": np.asarray(inputs["L1_attn_hash_proj2"]).reshape(H1 * 128, 128),
        "b1h": np.ascontiguousarray(np.asarray(inputs["L1_attn_hash_bias1"]).reshape(H1, 128).T),
        "b2h": np.ascontiguousarray(np.asarray(inputs["L1_attn_hash_bias2"]).reshape(H1, 128).T),
        "identb": np.eye(128, dtype=np.float32).astype(bf16),
        "negtriu": (-triu_incl).astype(bf16),
        "neglow": (-low_strict).astype(bf16),
        "ibig": (BIG * np.eye(128, dtype=np.float32)).astype(bf16),
        "inegbig": (-BIG * np.eye(128, dtype=np.float32)).astype(bf16),
    }
    return shared


def run_launchC(cfg, inputs, resB, debug=False):
    shared = prep_launchC_shared(cfg, inputs)
    S, KVH, T = cfg.S, cfg.KVH, cfg.T
    NK = S // 128
    in_maps = []
    for b in range(2):
        kfull = np.zeros((KVH * 128, S), bf16)
        vfull = np.zeros((S, KVH * 128), bf16)
        for r in range(4):
            idx = own_token_index(cfg, r)
            kfull[:, idx] = resB[b * 4 + r]["k1r_fm"]
            vfull[idx, :] = resB[b * 4 + r]["v1_tok"]
        vimg = np.ascontiguousarray(
            vfull.reshape(NK, 128, KVH, 128).transpose(2, 1, 0, 3).reshape(KVH * 128, S))
        for r in range(4):
            c = b * 4 + r
            m = dict(shared)
            m["x1_fm"] = resB[c]["x1_fm"]
            m["qr1_fm"] = resB[c]["qr1_fm"]
            m["g1_fm"] = resB[c]["g1_fm"]
            m["k1img"] = kfull
            m["v1img"] = vimg
            m["masks1"] = l0_masks(cfg, r, val=-384.0).astype(bf16)
            in_maps.append(m)
    nc = build_launchC(cfg, debug=debug)
    out = run_bass_kernel_spmd(nc, in_maps, core_ids=list(range(8)))
    return out.results
